# version 25
# baseline (speedup 1.0000x reference)
import math
import numpy as np
import concourse.bass as bass
import concourse.mybir as mybir
from concourse.bass_utils import run_bass_kernel_spmd

F32 = mybir.dt.float32
BF16 = mybir.dt.bfloat16
AF = mybir.ActivationFunctionType
ALU = mybir.AluOpType
AX = mybir.AxisListType

P = 128
D = 1024
FF = 4096
NT = 32
NCORES = 8
PI = math.pi
EPS = 1e-5

C_MEAN, C_VAR = 12, 13
C_SIG, C_HV, C_RC, C_SH = 0, 1, 2, 3
C_RSTD = 14
C_PI2 = 15
C_M0K, C_S0K = 16, 17

S_PER = 19
T_PER = 41
_TR_NUM = {0: 1, 1: 2, 2: 4, 3: 6, 4: 8, 5: 10, 6: 12, 7: 14}
_MM_NUM = {0: 3, 1: 5, 2: 7, 3: 9, 4: 11, 5: 13, 6: 15, 7: 16}


def _register_milestones():
    M = [dict() for _ in range(NT)]
    for i in range(NT):
        b = S_PER * i
        M[i]["sin"] = b + 3
        M[i]["ev0"] = b + 1
        M[i]["ev1"] = b + 2
        for k in range(2, 8):
            M[i][f"ev{k}"] = b + 2 + k
        M[i]["sinq"] = b + 10
        M[i]["qev"] = b + 11
        for g in range(8):
            M[i][f"relu{g}"] = b + 12 + g
        t = T_PER * i
        for k in range(8):
            M[i][f"tr{k}"] = t + _TR_NUM[k]
            M[i][f"mm{k}"] = t + _MM_NUM[k]
        M[i]["qtr"] = t + 17
        for g in range(8):
            M[i][f"m3g{g}"] = t + (18 + g if g <= 1 else 22 + 3 * (g - 2))
            M[i][f"m4hi{g}"] = t + (20 + 3 * g if g <= 6 else 40)
            M[i][f"m4lo{g}"] = t + (21 + 3 * g if g <= 6 else 41)
        M[i]["m12"] = t + 16
        M[i]["m4"] = t + 41
    return M


def build_core():
    nc = bass.Bass("TRN2", target_bir_lowering=False)
    x_d = nc.dram_tensor("x", [NT * P, D], F32, kind="ExternalInput").ap()
    out_d = nc.dram_tensor("out", [NT * P, D], F32, kind="ExternalOutput").ap()
    w2ta_d = nc.dram_tensor("w2ta", [FF, 512], F32, kind="ExternalInput").ap()
    w2tb_d = nc.dram_tensor("w2tb", [FF, 512], F32, kind="ExternalInput").ap()
    wft_d = nc.dram_tensor("wft", [P, (D // P) * D], F32, kind="ExternalInput").ap()
    w1a_d = nc.dram_tensor("w1a", [17, FF], F32, kind="ExternalInput").ap()
    dsh2_d = nc.dram_tensor("dsh2", [P, D], F32, kind="ExternalInput").ap()
    g1_d = nc.dram_tensor("g1r", [P, D], F32, kind="ExternalInput").ap()
    g2_d = nc.dram_tensor("g2r", [P, D], F32, kind="ExternalInput").ap()
    be2_d = nc.dram_tensor("be2r", [P, D], F32, kind="ExternalInput").ap()
    be1_d = nc.dram_tensor("be1r", [P, D], F32, kind="ExternalInput").ap()
    b2_d = nc.dram_tensor("b2r", [P, D], F32, kind="ExternalInput").ap()
    b116_d = nc.dram_tensor("b116r", [P, 16], F32, kind="ExternalInput").ap()

    from contextlib import ExitStack
    es = ExitStack()
    with es:
        sb = lambda n, s: es.enter_context(nc.sbuf_tensor(n, s, F32))
        sbh = lambda n, s: es.enter_context(nc.sbuf_tensor(n, s, BF16))
        psb = lambda n, s: es.enter_context(nc.psum_tensor(n, s, F32))
        sem = lambda n: es.enter_context(nc.semaphore(n))

        w2th_off = (nc.sbuf_base + 31) & ~31
        w2th = sbh("w2th_s", [P, (FF // P) * D])
        w2tl_off = (nc.sbuf_base + 31) & ~31
        w2tl = sbh("w2tl_s", [P, (FF // P) * D])
        wft = sb("wft_s", [P, (D // P) * D])
        w1a = sb("w1a_s", [17, FF])
        g1 = sb("g1_s", [P, D])
        g2 = sb("g2_s", [P, D])
        cb = sb("cb_s", [P, D])
        b116 = sb("b116_s", [P, 16])
        ident = sb("ident_s", [P, P])
        qt = sb("qt_s", [17, P])
        buf0 = sb("buf0_s", [P, D])
        buf1 = sb("buf1_s", [P, D])
        cbf = sb("c_s", [P, D])
        abf = sb("a_s", [P, D])
        hh_off = (nc.sbuf_base + 31) & ~31
        hh = sbh("hh_s", [P, 512])
        hl = sbh("hl_s", [P, 512])
        xnT = sb("xnT_s", [P, P])
        small = sb("small_s", [P, 67])
        xnT2 = nc.alloc_sbuf_tensor_at("xnT2_s", [P, P], F32, offset=hh_off)
        stgh = [nc.alloc_sbuf_tensor_at(f"stgh{c}", [P, 512], F32,
                                        offset=w2th_off + 2048 * (c + 1))
                for c in range(31)]
        stgl = [nc.alloc_sbuf_tensor_at(f"stgl{c}", [P, 512], F32,
                                        offset=w2tl_off + 2048 * (c + 1))
                for c in range(31)]
        sc = small[:, 0:49]
        st_ = small[:, 49:67]

        pm1 = psb("pm1", [P, 1024])
        pm4 = psb("pm4", [P, 1024])
        pm3 = psb("pm3", [P, 1024])
        ptrA = psb("ptrA", [P, 128])
        ptrB = psb("ptrB", [P, 128])

        s_x = sem("s_x")
        s_st = sem("s_st")
        s_wy = sem("s_wy")
        s_ck = [sem(f"s_ck{c}") for c in range(32)]
        s_vc = sem("s_vc")
        s_b2 = sem("s_b2")
        s_ws = sem("s_ws")
        ss_s = sem("ss")
        tn_s = sem("tn")
        sg_s = sem("sg")

        block = es.enter_context(nc.Block())
        M = _register_milestones()

        VC = {"n": 0, "last": None}
        LO_BAR = {}
        FB = {}

        def vop(inst):
            VC["last"] = inst
            return inst

        def vbar(v):
            if VC["last"] is not None:
                VC["n"] += 1
                VC["last"].then_inc(s_vc, 1)
                v.wait_ge(s_vc, VC["n"])
                VC["last"] = None

        @block.scalar
        def _(s):
            s.dma_start(out=buf1[:], in_=b2_d).then_inc(s_ws, 16)

        mean = st_[:, C_MEAN:C_MEAN + 1]
        rstd = st_[:, C_RSTD:C_RSTD + 1]

        def rstd_chain(v, var, out):
            sig = st_[:, C_SIG:C_SIG + 1]
            hv = st_[:, C_HV:C_HV + 1]
            rc = st_[:, C_RC:C_RC + 1]
            sh = st_[:, C_SH:C_SH + 1]
            v.tensor_scalar(sig, var, 1.0 + EPS, 0.5, ALU.add, ALU.mult)
            vop(v.tensor_scalar(hv, var, EPS, 0.5, ALU.add, ALU.mult))
            vbar(v)
            for _ in range(3):
                v.reciprocal(rc, sig)
                vop(v.tensor_scalar(sh, sig, 0.5, None, ALU.mult))
                vbar(v)
                vop(v.scalar_tensor_tensor(sig, rc, hv, sh, ALU.mult, ALU.add))
                vbar(v)
            vop(v.reciprocal(out, sig))
            vbar(v)

        def ln_stats(v, src, mv_pair, var, out):
            v.bn_stats(st_[:, 0:6], src[:, 0:512])
            vop(v.bn_stats(st_[:, 6:12], src[:, 512:1024]))
            vbar(v)
            vop(v.bn_aggr(mv_pair, st_[:, 0:12]))
            vbar(v)
            rstd_chain(v, var, out)

        def emit_masks(v, i, B):
            vop(v.tensor_scalar(cbf[:], B[:], PI / 2, None, ALU.is_gt))
            vbar(v)
            vop(v.scalar_tensor_tensor(cbf[:], B[:], -3 * PI / 2, cbf[:],
                                       ALU.is_lt, ALU.subtract))
            vbar(v)
            vop(v.scalar_tensor_tensor(cbf[:], cbf[:], 2 * PI, B[:],
                                       ALU.mult, ALU.add))
            vbar(v)
            M[i]["wrap"] = VC["n"]

        @block.vector
        def _(v):
            v.memset(ident[:], 1.0)
            v.memset(st_[:, C_PI2:C_PI2 + 1], PI / 2)
            v.memset(sc[:, 16:17], 1.0)
            vop(v.memset(qt[0:17, :], 1.0))
            vbar(v)

            def emit_split_p(c):
                sh = stgh[c][:] if c < 31 else abf[:, 0:512]
                sl = stgl[c][:] if c < 31 else abf[:, 512:1024]
                v.wait_ge(s_ck[c], 32)
                v.tensor_copy(w2th[:, D * c:D * c + 512], sh)
                vop(v.tensor_copy(w2th[:, D * c + 512:D * (c + 1)], sl))
                vbar(v)
                v.tensor_tensor(w2tl[:, D * c:D * c + 512], sh,
                                w2th[:, D * c:D * c + 512], ALU.subtract)
                vop(v.tensor_tensor(w2tl[:, D * c + 512:D * (c + 1)], sl,
                                    w2th[:, D * c + 512:D * (c + 1)],
                                    ALU.subtract))
                vbar(v)
                LO_BAR[c] = VC["n"]

            v.wait_ge(s_x, 16)
            ln_stats(v, buf0, sc[:, 45:47], sc[:, 46:47], sc[:, 47:48])
            vop(v.tensor_scalar(buf0[:], buf0[:], sc[:, 45:46], sc[:, 47:48],
                                ALU.subtract, ALU.mult))
            vbar(v)
            M[0]["xn"] = VC["n"]
            v.tensor_copy(st_[:, C_M0K:C_M0K + 1], sc[:, 45:46])
            vop(v.tensor_copy(st_[:, C_S0K:C_S0K + 1], st_[:, C_SIG:C_SIG + 1]))
            vbar(v)
            emit_masks(v, 0, buf0)
            for c in range(6):
                emit_split_p(c)
            v.wait_ge(s_wy, 6 * 16)
            v.wait_ge(s_ws, 16)
            for k in range(D // P):
                v.tensor_tensor(wft[:, D * k:D * (k + 1)],
                                wft[:, D * k:D * (k + 1)], g2[:], ALU.add)
            vop(v.tensor_tensor(cb[:], cb[:], buf1[:], ALU.add))
            vbar(v)
            FB["fold"] = VC["n"]

        @block.gpsimd
        def _(g):
            g.dma_start(out=wft[:], in_=wft_d).then_inc(s_wy, 16)
            g.dma_start(out=w1a[:], in_=w1a_d).then_inc(s_wy, 16)
            g.dma_start(out=g1[:], in_=g1_d).then_inc(s_wy, 16)
            g.dma_start(out=cb[:], in_=be1_d).then_inc(s_wy, 16)
            g.dma_start(out=b116[:], in_=b116_d).then_inc(s_wy, 16)
            g.dma_start(out=g2[:], in_=dsh2_d).then_inc(s_wy, 16)
            g.wait_ge(s_vc, 1)
            g.affine_select(ident[:], ident[:], [[1, P]], ALU.is_equal, 0.0,
                            base=0, channel_multiplier=-1).then_inc(sg_s, 1)
            g.wait_ge(s_vc, FB["fold"])
            g.dma_start(out=g2[:], in_=g2_d).then_inc(s_wy, 16)

        @block.vector
        def _(v):
            def emit_split(c):
                sh = stgh[c][:] if c < 31 else abf[:, 0:512]
                sl = stgl[c][:] if c < 31 else abf[:, 512:1024]
                v.wait_ge(s_ck[c], 32)
                v.tensor_copy(w2th[:, D * c:D * c + 512], sh)
                vop(v.tensor_copy(w2th[:, D * c + 512:D * (c + 1)], sl))
                vbar(v)
                v.tensor_tensor(w2tl[:, D * c:D * c + 512], sh,
                                w2th[:, D * c:D * c + 512], ALU.subtract)
                vop(v.tensor_tensor(w2tl[:, D * c + 512:D * (c + 1)], sl,
                                    w2th[:, D * c + 512:D * (c + 1)],
                                    ALU.subtract))
                vbar(v)
                LO_BAR[c] = VC["n"]

            for i in range(NT):
                mi = M[i]
                B = buf0 if i % 2 == 0 else buf1
                Bn = buf1 if i % 2 == 0 else buf0
                v.wait_ge(ss_s, mi["sin"])
                for g in range(16):
                    vop(v.tensor_tensor_scan(cbf[:, 64 * g:64 * (g + 1)],
                                             cbf[:, 64 * g:64 * (g + 1)],
                                             cbf[:, 64 * g:64 * (g + 1)],
                                             1.0, ALU.mult, ALU.bypass))
                vbar(v)
                vop(v.tensor_tensor(cbf[:], B[:], cbf[:], ALU.add))
                vbar(v)
                mi["z1"] = VC["n"]
                ln_stats(v, cbf, st_[:, C_MEAN:C_MEAN + 2], st_[:, C_VAR:C_VAR + 1], rstd)
                vop(v.scalar_tensor_tensor(cbf[:], cbf[:], mean, g1[:],
                                           ALU.subtract, ALU.mult))
                vbar(v)
                mi["x1"] = VC["n"]
                vop(v.tensor_scalar(sc[:, 0:16], cbf[:, 0:16], rstd, None,
                                    ALU.mult))
                vbar(v)
                vop(v.tensor_tensor(sc[:, 0:16], sc[:, 0:16], b116[:],
                                    ALU.add))
                vbar(v)
                v.tensor_scalar(sc[:, 17:33], sc[:, 0:16], PI / 2, PI,
                                ALU.add, ALU.is_gt)
                vop(v.tensor_scalar(sc[:, 33:49], sc[:, 0:16], PI / 2, -PI,
                                    ALU.add, ALU.is_lt))
                vbar(v)
                vop(v.tensor_tensor(sc[:, 17:33], sc[:, 17:33], sc[:, 33:49],
                                    ALU.subtract))
                vbar(v)
                vop(v.scalar_tensor_tensor(sc[:, 0:16], sc[:, 17:33], -2 * PI,
                                           sc[:, 0:16], ALU.mult, ALU.add))
                vbar(v)
                mi["wq"] = VC["n"]
                v.wait_ge(ss_s, mi["sinq"])
                vop(v.tensor_tensor_scan(
                    sc[:, 17:33], sc[:, 33:49], sc[:, 33:49], 1.0,
                    ALU.mult, ALU.bypass))
                vbar(v)
                mi["q"] = VC["n"]
                if i == 0:
                    for c in range(6, 10):
                        emit_split(c)
                v.wait_ge(tn_s, mi["m12"])
                vop(v.scalar_tensor_tensor(B[:], B[:], st_[:, C_S0K:C_S0K + 1],
                                           pm1[:], ALU.mult, ALU.add))
                vbar(v)
                vop(v.tensor_scalar(B[:], B[:], st_[:, C_M0K:C_M0K + 1],
                                    None, ALU.add))
                vbar(v)
                mi["s"] = VC["n"]
                def emit_hl(g):
                    hp = g % 2
                    v.wait_ge(ss_s, mi[f"relu{g}"])
                    if g >= 1:
                        v.wait_ge(tn_s, mi[f"m4lo{g - 1}"])
                    vop(v.scalar_tensor_tensor(
                        hl[:], pm3[:, 512 * hp:512 * (hp + 1)], 0.0, hh[:],
                        ALU.max, ALU.subtract))
                    vbar(v)
                    mi[f"hl{g}"] = VC["n"]

                if i == 0:
                    for c in range(10, 14):
                        emit_split(c)
                emit_hl(0)
                if i == 0:
                    for c in range(14, 18):
                        emit_split(c)
                emit_hl(1)
                if i + 1 < NT:
                    v.wait_ge(s_x, 16 * (i + 2))
                    v.bn_stats(st_[:, 0:6], Bn[:, 0:512])
                    vop(v.bn_stats(st_[:, 6:12], Bn[:, 512:1024]))
                    vbar(v)
                    vop(v.bn_aggr(sc[:, 45:47], st_[:, 0:12]))
                    vbar(v)
                if i == 0:
                    for c in range(18, 22):
                        emit_split(c)
                emit_hl(2)
                if i == 0:
                    for c in range(22, 26):
                        emit_split(c)
                emit_hl(3)
                if i + 1 < NT:
                    rstd_chain(v, sc[:, 46:47], sc[:, 47:48])
                if i == 0:
                    for c in range(26, 30):
                        emit_split(c)
                emit_hl(4)
                if i == 0:
                    for c in range(30, 32):
                        emit_split(c)
                emit_hl(5)
                if i + 1 < NT:
                    vop(v.tensor_scalar(Bn[:], Bn[:], sc[:, 45:46],
                                        sc[:, 47:48], ALU.subtract, ALU.mult))
                    vbar(v)
                    M[i + 1]["xn"] = VC["n"]
                    v.tensor_copy(st_[:, C_M0K:C_M0K + 1], sc[:, 45:46])
                    vop(v.tensor_copy(st_[:, C_S0K:C_S0K + 1],
                                      st_[:, C_SIG:C_SIG + 1]))
                    vbar(v)
                emit_hl(6)
                emit_hl(7)
                v.wait_ge(tn_s, mi["m4"])
                if i >= 1:
                    v.wait_ge(s_st, 16 * i)
                vop(v.scalar_tensor_tensor(abf[:], cbf[:], rstd, pm4[:],
                                           ALU.mult, ALU.add))
                vbar(v)
                mi["z2"] = VC["n"]
                if i + 1 < NT:
                    emit_masks(v, i + 1, Bn)
                ln_stats(v, abf, st_[:, C_MEAN:C_MEAN + 2], st_[:, C_VAR:C_VAR + 1], rstd)
                if i == 0:
                    v.wait_ge(s_wy, 7 * 16)
                vop(v.scalar_tensor_tensor(abf[:], abf[:], mean, g2[:],
                                           ALU.subtract, ALU.mult))
                vbar(v)
                v.wait_ge(s_b2, 16 * (i + 1))
                vop(v.scalar_tensor_tensor(abf[:], abf[:], rstd, B[:],
                                           ALU.mult, ALU.add))
                vbar(v)
                mi["out"] = VC["n"]

        @block.gpsimd
        def _(g):
            for i in range(NT):
                B_t = buf0 if i % 2 == 0 else buf1
                g.wait_ge(s_vc, M[i]["s"])
                g.dma_start(out=B_t[:], in_=be2_d,
                            accum_op=ALU.add).then_inc(s_b2, 16)

        @block.scalar
        def _(s):
            for i in range(NT):
                mi = M[i]
                for k in range(2):
                    s.wait_ge(tn_s, mi[f"tr{k}"])
                    if k == 0 and i >= 1:
                        s.wait_ge(s_vc, M[i - 1]["hl7"])
                        s.wait_ge(tn_s, M[i - 1]["m4hi7"])
                    src = ptrA if k % 2 == 0 else ptrB
                    dst = xnT2 if k % 2 == 0 else xnT
                    s.copy(dst[:], src[:]).then_inc(ss_s, 1)
                s.wait_ge(s_vc, mi["wrap"])
                s.activation(cbf[:], cbf[:], AF.Sin,
                             bias=st_[:, C_PI2:C_PI2 + 1]).then_inc(ss_s, 1)
                for k in range(2, 8):
                    s.wait_ge(tn_s, mi[f"tr{k}"])
                    src = ptrA if k % 2 == 0 else ptrB
                    dst = xnT2 if k % 2 == 0 else xnT
                    s.copy(dst[:], src[:]).then_inc(ss_s, 1)
                s.wait_ge(s_vc, mi["wq"])
                s.activation(sc[:, 33:49], sc[:, 0:16], AF.Sin,
                             bias=st_[:, C_PI2:C_PI2 + 1]).then_inc(ss_s, 1)
                s.wait_ge(tn_s, mi["qtr"])
                s.copy(qt[0:17, :], ptrA[0:17, :]).then_inc(ss_s, 1)
                for g in range(8):
                    s.wait_ge(tn_s, mi[f"m3g{g}"])
                    if g >= 1:
                        s.wait_ge(tn_s, mi[f"m4hi{g - 1}"])
                        s.wait_ge(s_vc, mi[f"hl{g - 1}"])
                    elif i >= 1:
                        s.wait_ge(tn_s, M[i - 1]["m4hi7"])
                        s.wait_ge(s_vc, M[i - 1]["hl7"])
                    h = g % 2
                    s.activation(hh[:], pm3[:, 512 * h:512 * (h + 1)],
                                 AF.Relu).then_inc(ss_s, 1)

        @block.tensor
        def _(t):
            t.wait_ge(sg_s, 1)
            for i in range(NT):
                mi = M[i]
                Bx = buf0 if i % 2 == 0 else buf1
                t.wait_ge(s_vc, mi["xn"])
                if i >= 1:
                    t.wait_ge(ss_s, M[i - 1]["qev"])
                    t.wait_ge(ss_s, M[i - 1]["ev7"])
                    t.wait_ge(s_vc, M[i - 1]["s"])

                def tr(k):
                    dst = ptrA if k % 2 == 0 else ptrB
                    if k >= 2:
                        t.wait_ge(ss_s, mi[f"ev{k - 2}"])
                    t.transpose(dst[:], Bx[:, P * k:P * (k + 1)],
                                ident[:]).then_inc(tn_s, 1)

                def m1mm(k):
                    t.wait_ge(ss_s, mi[f"ev{k}"])
                    xs = xnT2 if k % 2 == 0 else xnT
                    t.matmul(pm1[:, 0:512], xs[:], wft[:, D * k:D * k + 512],
                             start=(k == 0), stop=(k == 7))
                    t.matmul(pm1[:, 512:1024], xs[:],
                             wft[:, D * k + 512:D * (k + 1)],
                             start=(k == 0), stop=(k == 7)).then_inc(tn_s, 1)

                tr(0)
                tr(1)
                if i == 0:
                    t.wait_ge(s_vc, FB["fold"])
                else:
                    t.wait_ge(s_vc, M[i - 1]["z2"])
                t.matmul(pm4[:, 0:512], qt[0:1, :], cb[0:1, 0:512],
                         start=True, stop=False)
                t.matmul(pm4[:, 512:1024], qt[0:1, :], cb[0:1, 512:1024],
                         start=True, stop=False)
                for k in range(8):
                    m1mm(k)
                    if k + 2 < 8:
                        tr(k + 2)
                t.wait_ge(s_vc, mi["q"])
                t.transpose(ptrA[0:17, :], sc[:, 16:33],
                            ident[:]).then_inc(tn_s, 1)

                def m3grp(g):
                    h = g % 2
                    for r in range(4):
                        c = 4 * g + r
                        mm = t.matmul(
                            pm3[:, 512 * h + 128 * r:512 * h + 128 * (r + 1)],
                            w1a[0:17, 128 * c:128 * (c + 1)], qt[0:17, :],
                            start=(r == 0), stop=(r == 3))
                    mm.then_inc(tn_s, 1)

                t.wait_ge(ss_s, mi["qev"])
                if i >= 1:
                    t.wait_ge(ss_s, M[i - 1]["relu7"])
                    t.wait_ge(s_vc, M[i - 1]["hl7"])
                m3grp(0)
                m3grp(1)
                for g in range(8):
                    t.wait_ge(ss_s, mi[f"relu{g}"])
                    if i == 0:
                        t.wait_ge(s_vc, LO_BAR[4 * g + 3])
                    for r in range(4):
                        c = 4 * g + r
                        hs = hh[:, 128 * r:128 * (r + 1)]
                        t.matmul(pm4[:, 0:512], hs, w2th[:, D * c:D * c + 512],
                                 start=False, stop=False)
                        t.matmul(pm4[:, 512:1024], hs,
                                 w2th[:, D * c + 512:D * (c + 1)],
                                 start=False, stop=False)
                        t.matmul(pm4[:, 0:512], hs, w2tl[:, D * c:D * c + 512],
                                 start=False, stop=False)
                        mm2 = t.matmul(pm4[:, 512:1024], hs,
                                       w2tl[:, D * c + 512:D * (c + 1)],
                                       start=False, stop=False)
                    mm2.then_inc(tn_s, 1)
                    t.wait_ge(s_vc, mi[f"hl{g}"])
                    for r in range(4):
                        c = 4 * g + r
                        ls = hl[:, 128 * r:128 * (r + 1)]
                        t.matmul(pm4[:, 0:512], ls, w2th[:, D * c:D * c + 512],
                                 start=False, stop=(g == 7 and r == 3))
                        mm3 = t.matmul(pm4[:, 512:1024], ls,
                                       w2th[:, D * c + 512:D * (c + 1)],
                                       start=False, stop=(g == 7 and r == 3))
                    mm3.then_inc(tn_s, 1)
                    if g + 2 <= 7:
                        m3grp(g + 2)

        @block.sync
        def _(y):
            y.dma_start(out=buf0[:], in_=x_d[0:P, :]).then_inc(s_x, 16)
            for c in range(32):
                dh = stgh[c][:] if c < 31 else abf[:, 0:512]
                dl = stgl[c][:] if c < 31 else abf[:, 512:1024]
                if c == 6:
                    y.wait_ge(s_vc, FB["fold"])
                    y.dma_start(out=buf1[:],
                                in_=x_d[P:2 * P, :]).then_inc(s_x, 16)
                if c >= 6:
                    y.wait_ge(s_vc, LO_BAR[c - 6])
                y.dma_start(out=dh,
                            in_=w2ta_d[P * c:P * (c + 1), :]).then_inc(
                    s_ck[c], 16)
                y.dma_start(out=dl,
                            in_=w2tb_d[P * c:P * (c + 1), :]).then_inc(
                    s_ck[c], 16)
            for i in range(NT):
                if 1 <= i < NT - 1:
                    Bn = buf1 if i % 2 == 0 else buf0
                    y.wait_ge(s_vc, M[i - 1]["out"])
                    y.wait_ge(tn_s, M[i - 1]["tr7"])
                    y.dma_start(out=Bn[:],
                                in_=x_d[P * (i + 1):P * (i + 2), :]).then_inc(
                        s_x, 16)
                y.wait_ge(s_vc, M[i]["out"])
                y.dma_start(out=out_d[P * i:P * (i + 1), :],
                            in_=abf[:]).then_inc(s_st, 16)
            y.wait_ge(s_st, 16 * NT)

    return nc


def _host_layouts(w1, b1, w2, g1v, be1v, g2v, be2v, b2v, wf, dsh):
    f32 = np.float32
    w2t = np.asarray(w2, f32).T
    w2ta = np.ascontiguousarray(w2t[:, 0:512])
    w2tb = np.ascontiguousarray(w2t[:, 512:1024])
    wft = np.ascontiguousarray(
        np.asarray(wf, f32).T.reshape(8, 128, 1024)
        .transpose(1, 0, 2).reshape(128, 8 * 1024))
    w1a = np.ascontiguousarray(np.concatenate(
        [np.asarray(b1, f32)[None, :], np.asarray(w1, f32).T], axis=0))
    d = np.asarray(dsh, f32)
    dsh2 = np.ascontiguousarray(np.concatenate([d, d], axis=0))
    rep = lambda v: np.ascontiguousarray(
        np.broadcast_to(np.asarray(v, f32)[None, :], (P, D)))
    return {"w2ta": w2ta, "w2tb": w2tb, "wft": wft, "w1a": w1a, "dsh2": dsh2,
            "g1r": rep(g1v), "g2r": rep(g2v), "be2r": rep(be2v),
            "be1r": rep(be1v), "b2r": rep(b2v),
            "b116r": np.ascontiguousarray(
                np.broadcast_to(np.asarray(be1v, f32)[None, :16], (P, 16)))}


def kernel(L0_attn_x, L0_attn_attn_circuit_params, L0_attn_ffn_circuit_params,
           L0_attn_w1, L0_attn_b1, L0_attn_w2, L0_attn_b2,
           L0_attn_gamma1, L0_attn_beta1, L0_attn_gamma2, L0_attn_beta2,
           L0_ffn_weight, L0_ffn_disha):
    x = np.ascontiguousarray(np.asarray(L0_attn_x, np.float32).reshape(-1, D))
    weights = _host_layouts(L0_attn_w1, L0_attn_b1, L0_attn_w2,
                            L0_attn_gamma1, L0_attn_beta1,
                            L0_attn_gamma2, L0_attn_beta2,
                            L0_attn_b2, L0_ffn_weight, L0_ffn_disha)
    rows = x.shape[0] // NCORES
    in_maps = []
    for c in range(NCORES):
        m = dict(weights)
        m["x"] = np.ascontiguousarray(x[rows * c:rows * (c + 1)])
        in_maps.append(m)
    nc = build_core()
    res = run_bass_kernel_spmd(nc, in_maps, core_ids=list(range(NCORES)))
    out = np.concatenate([np.asarray(r["out"]) for r in res.results], axis=0)
    return np.ascontiguousarray(out.reshape(8, 4096, 1024).astype(np.float32))


# version 26
# speedup vs baseline: 1.0267x; 1.0267x over previous
import math
import numpy as np
import concourse.bass as bass
import concourse.mybir as mybir
from concourse.bass_utils import run_bass_kernel_spmd

F32 = mybir.dt.float32
BF16 = mybir.dt.bfloat16
AF = mybir.ActivationFunctionType
ALU = mybir.AluOpType
AX = mybir.AxisListType

P = 128
D = 1024
FF = 4096
NT = 32
NCORES = 8
PI = math.pi
EPS = 1e-5

C_MEAN, C_VAR = 12, 13
C_SIG, C_HV, C_RC, C_SH = 0, 1, 2, 3
C_RSTD = 14
C_PI2 = 15
C_M0K, C_S0K = 16, 17

S_PER = 19
T_PER = 41
_TR_NUM = {0: 1, 1: 2, 2: 4, 3: 6, 4: 8, 5: 10, 6: 12, 7: 14}
_MM_NUM = {0: 3, 1: 5, 2: 7, 3: 9, 4: 11, 5: 13, 6: 15, 7: 16}


def _register_milestones():
    M = [dict() for _ in range(NT)]
    for i in range(NT):
        b = S_PER * i
        M[i]["sin"] = b + 1
        for k in range(8):
            M[i][f"ev{k}"] = b + 2 + k
        M[i]["sinq"] = b + 10
        M[i]["qev"] = b + 11
        for g in range(8):
            M[i][f"relu{g}"] = b + 12 + g
        t = T_PER * i
        for k in range(8):
            M[i][f"tr{k}"] = t + _TR_NUM[k]
            M[i][f"mm{k}"] = t + _MM_NUM[k]
        M[i]["qtr"] = t + 17
        for g in range(8):
            M[i][f"m3g{g}"] = t + (18 + g if g <= 1 else 22 + 3 * (g - 2))
            M[i][f"m4hi{g}"] = t + (20 + 3 * g if g <= 6 else 40)
            M[i][f"m4lo{g}"] = t + (21 + 3 * g if g <= 6 else 41)
        M[i]["m12"] = t + 16
        M[i]["m4"] = t + 41
    return M


def build_core():
    nc = bass.Bass("TRN2", target_bir_lowering=False)
    x_d = nc.dram_tensor("x", [NT * P, D], F32, kind="ExternalInput").ap()
    out_d = nc.dram_tensor("out", [NT * P, D], F32, kind="ExternalOutput").ap()
    w2ta_d = nc.dram_tensor("w2ta", [FF, 512], F32, kind="ExternalInput").ap()
    w2tb_d = nc.dram_tensor("w2tb", [FF, 512], F32, kind="ExternalInput").ap()
    wft_d = nc.dram_tensor("wft", [P, (D // P) * D], F32, kind="ExternalInput").ap()
    w1a_d = nc.dram_tensor("w1a", [17, FF], F32, kind="ExternalInput").ap()
    dsh2_d = nc.dram_tensor("dsh2", [P, D], F32, kind="ExternalInput").ap()
    g1_d = nc.dram_tensor("g1r", [P, D], F32, kind="ExternalInput").ap()
    g2_d = nc.dram_tensor("g2r", [P, D], F32, kind="ExternalInput").ap()
    be2_d = nc.dram_tensor("be2r", [P, D], F32, kind="ExternalInput").ap()
    be1_d = nc.dram_tensor("be1r", [P, D], F32, kind="ExternalInput").ap()
    b2_d = nc.dram_tensor("b2r", [P, D], F32, kind="ExternalInput").ap()
    b116_d = nc.dram_tensor("b116r", [P, 16], F32, kind="ExternalInput").ap()

    from contextlib import ExitStack
    es = ExitStack()
    with es:
        sb = lambda n, s: es.enter_context(nc.sbuf_tensor(n, s, F32))
        sbh = lambda n, s: es.enter_context(nc.sbuf_tensor(n, s, BF16))
        psb = lambda n, s: es.enter_context(nc.psum_tensor(n, s, F32))
        sem = lambda n: es.enter_context(nc.semaphore(n))

        w2th_off = (nc.sbuf_base + 31) & ~31
        w2th = sbh("w2th_s", [P, (FF // P) * D])
        w2tl_off = (nc.sbuf_base + 31) & ~31
        w2tl = sbh("w2tl_s", [P, (FF // P) * D])
        wft = sb("wft_s", [P, (D // P) * D])
        w1a = sb("w1a_s", [17, FF])
        g1 = sb("g1_s", [P, D])
        g2 = sb("g2_s", [P, D])
        cb = sb("cb_s", [P, D])
        b116 = sb("b116_s", [P, 16])
        ident = sb("ident_s", [P, P])
        qt = sb("qt_s", [17, P])
        buf0 = sb("buf0_s", [P, D])
        buf1 = sb("buf1_s", [P, D])
        cbf = sb("c_s", [P, D])
        abf = sb("a_s", [P, D])
        hh_off = (nc.sbuf_base + 31) & ~31
        hh = sbh("hh_s", [P, 512])
        hl = sbh("hl_s", [P, 512])
        xnT = sb("xnT_s", [P, P])
        small = sb("small_s", [P, 67])
        xnT2 = nc.alloc_sbuf_tensor_at("xnT2_s", [P, P], F32, offset=hh_off)
        stgh = [nc.alloc_sbuf_tensor_at(f"stgh{c}", [P, 512], F32,
                                        offset=w2th_off + 2048 * (c + 1))
                for c in range(31)]
        stgl = [nc.alloc_sbuf_tensor_at(f"stgl{c}", [P, 512], F32,
                                        offset=w2tl_off + 2048 * (c + 1))
                for c in range(31)]
        sc = small[:, 0:49]
        st_ = small[:, 49:67]

        pm1 = psb("pm1", [P, 1024])
        pm4 = psb("pm4", [P, 1024])
        pm3 = psb("pm3", [P, 1024])
        ptrA = psb("ptrA", [P, 128])
        ptrB = psb("ptrB", [P, 128])

        s_x = sem("s_x")
        s_st = sem("s_st")
        s_wy = sem("s_wy")
        s_ck = [sem(f"s_ck{c}") for c in range(32)]
        s_vc = sem("s_vc")
        s_b2 = sem("s_b2")
        s_ws = sem("s_ws")
        ss_s = sem("ss")
        tn_s = sem("tn")
        sg_s = sem("sg")

        block = es.enter_context(nc.Block())
        M = _register_milestones()

        VC = {"n": 0, "last": None}
        LO_BAR = {}
        FB = {}

        def vop(inst):
            VC["last"] = inst
            return inst

        def vbar(v):
            if VC["last"] is not None:
                VC["n"] += 1
                VC["last"].then_inc(s_vc, 1)
                v.wait_ge(s_vc, VC["n"])
                VC["last"] = None

        @block.scalar
        def _(s):
            s.dma_start(out=buf1[:], in_=b2_d).then_inc(s_ws, 16)

        mean = st_[:, C_MEAN:C_MEAN + 1]
        rstd = st_[:, C_RSTD:C_RSTD + 1]

        def rstd_chain(v, var, out):
            sig = st_[:, C_SIG:C_SIG + 1]
            hv = st_[:, C_HV:C_HV + 1]
            rc = st_[:, C_RC:C_RC + 1]
            sh = st_[:, C_SH:C_SH + 1]
            v.tensor_scalar(sig, var, 1.0 + EPS, 0.5, ALU.add, ALU.mult)
            vop(v.tensor_scalar(hv, var, EPS, 0.5, ALU.add, ALU.mult))
            vbar(v)
            for _ in range(3):
                v.reciprocal(rc, sig)
                vop(v.tensor_scalar(sh, sig, 0.5, None, ALU.mult))
                vbar(v)
                vop(v.scalar_tensor_tensor(sig, rc, hv, sh, ALU.mult, ALU.add))
                vbar(v)
            vop(v.reciprocal(out, sig))
            vbar(v)

        def ln_stats(v, src, mv_pair, var, out):
            v.bn_stats(st_[:, 0:6], src[:, 0:512])
            vop(v.bn_stats(st_[:, 6:12], src[:, 512:1024]))
            vbar(v)
            vop(v.bn_aggr(mv_pair, st_[:, 0:12]))
            vbar(v)
            rstd_chain(v, var, out)

        def emit_masks(v, i, B):
            vop(v.tensor_scalar(cbf[:], B[:], PI / 2, None, ALU.is_gt))
            vbar(v)
            vop(v.scalar_tensor_tensor(cbf[:], B[:], -3 * PI / 2, cbf[:],
                                       ALU.is_lt, ALU.subtract))
            vbar(v)
            vop(v.scalar_tensor_tensor(cbf[:], cbf[:], 2 * PI, B[:],
                                       ALU.mult, ALU.add))
            vbar(v)
            M[i]["wrap"] = VC["n"]

        @block.vector
        def _(v):
            v.memset(ident[:], 1.0)
            v.memset(st_[:, C_PI2:C_PI2 + 1], PI / 2)
            v.memset(sc[:, 16:17], 1.0)
            vop(v.memset(qt[0:17, :], 1.0))
            vbar(v)

            def emit_split_p(c):
                sh = stgh[c][:] if c < 31 else abf[:, 0:512]
                sl = stgl[c][:] if c < 31 else abf[:, 512:1024]
                v.wait_ge(s_ck[c], 32)
                v.tensor_copy(w2th[:, D * c:D * c + 512], sh)
                vop(v.tensor_copy(w2th[:, D * c + 512:D * (c + 1)], sl))
                vbar(v)
                v.tensor_tensor(w2tl[:, D * c:D * c + 512], sh,
                                w2th[:, D * c:D * c + 512], ALU.subtract)
                vop(v.tensor_tensor(w2tl[:, D * c + 512:D * (c + 1)], sl,
                                    w2th[:, D * c + 512:D * (c + 1)],
                                    ALU.subtract))
                vbar(v)
                LO_BAR[c] = VC["n"]

            v.wait_ge(s_x, 16)
            ln_stats(v, buf0, sc[:, 45:47], sc[:, 46:47], sc[:, 47:48])
            vop(v.tensor_scalar(buf0[:], buf0[:], sc[:, 45:46], sc[:, 47:48],
                                ALU.subtract, ALU.mult))
            vbar(v)
            M[0]["xn"] = VC["n"]
            v.tensor_copy(st_[:, C_M0K:C_M0K + 1], sc[:, 45:46])
            vop(v.tensor_copy(st_[:, C_S0K:C_S0K + 1], st_[:, C_SIG:C_SIG + 1]))
            vbar(v)
            emit_masks(v, 0, buf0)
            for c in range(6):
                emit_split_p(c)
            v.wait_ge(s_wy, 6 * 16)
            v.wait_ge(s_ws, 16)
            for k in range(D // P):
                v.tensor_tensor(wft[:, D * k:D * (k + 1)],
                                wft[:, D * k:D * (k + 1)], g2[:], ALU.add)
            vop(v.tensor_tensor(cb[:], cb[:], buf1[:], ALU.add))
            vbar(v)
            FB["fold"] = VC["n"]

        @block.gpsimd
        def _(g):
            g.dma_start(out=wft[:], in_=wft_d).then_inc(s_wy, 16)
            g.dma_start(out=w1a[:], in_=w1a_d).then_inc(s_wy, 16)
            g.dma_start(out=g1[:], in_=g1_d).then_inc(s_wy, 16)
            g.dma_start(out=cb[:], in_=be1_d).then_inc(s_wy, 16)
            g.dma_start(out=b116[:], in_=b116_d).then_inc(s_wy, 16)
            g.dma_start(out=g2[:], in_=dsh2_d).then_inc(s_wy, 16)
            g.wait_ge(s_vc, 1)
            g.affine_select(ident[:], ident[:], [[1, P]], ALU.is_equal, 0.0,
                            base=0, channel_multiplier=-1).then_inc(sg_s, 1)
            g.wait_ge(s_vc, FB["fold"])
            g.dma_start(out=g2[:], in_=g2_d).then_inc(s_wy, 16)

        @block.vector
        def _(v):
            def emit_split(c):
                sh = stgh[c][:] if c < 31 else abf[:, 0:512]
                sl = stgl[c][:] if c < 31 else abf[:, 512:1024]
                v.wait_ge(s_ck[c], 32)
                v.tensor_copy(w2th[:, D * c:D * c + 512], sh)
                vop(v.tensor_copy(w2th[:, D * c + 512:D * (c + 1)], sl))
                vbar(v)
                v.tensor_tensor(w2tl[:, D * c:D * c + 512], sh,
                                w2th[:, D * c:D * c + 512], ALU.subtract)
                vop(v.tensor_tensor(w2tl[:, D * c + 512:D * (c + 1)], sl,
                                    w2th[:, D * c + 512:D * (c + 1)],
                                    ALU.subtract))
                vbar(v)
                LO_BAR[c] = VC["n"]

            for i in range(NT):
                mi = M[i]
                B = buf0 if i % 2 == 0 else buf1
                Bn = buf1 if i % 2 == 0 else buf0
                v.wait_ge(ss_s, mi["sin"])
                for g in range(16):
                    vop(v.tensor_tensor_scan(cbf[:, 64 * g:64 * (g + 1)],
                                             cbf[:, 64 * g:64 * (g + 1)],
                                             cbf[:, 64 * g:64 * (g + 1)],
                                             1.0, ALU.mult, ALU.bypass))
                vbar(v)
                vop(v.tensor_tensor(cbf[:], B[:], cbf[:], ALU.add))
                vbar(v)
                mi["z1"] = VC["n"]
                ln_stats(v, cbf, st_[:, C_MEAN:C_MEAN + 2], st_[:, C_VAR:C_VAR + 1], rstd)
                vop(v.scalar_tensor_tensor(cbf[:], cbf[:], mean, g1[:],
                                           ALU.subtract, ALU.mult))
                vbar(v)
                mi["x1"] = VC["n"]
                vop(v.tensor_scalar(sc[:, 0:16], cbf[:, 0:16], rstd, None,
                                    ALU.mult))
                vbar(v)
                vop(v.tensor_tensor(sc[:, 0:16], sc[:, 0:16], b116[:],
                                    ALU.add))
                vbar(v)
                v.tensor_scalar(sc[:, 17:33], sc[:, 0:16], PI / 2, PI,
                                ALU.add, ALU.is_gt)
                vop(v.tensor_scalar(sc[:, 33:49], sc[:, 0:16], PI / 2, -PI,
                                    ALU.add, ALU.is_lt))
                vbar(v)
                vop(v.tensor_tensor(sc[:, 17:33], sc[:, 17:33], sc[:, 33:49],
                                    ALU.subtract))
                vbar(v)
                vop(v.scalar_tensor_tensor(sc[:, 0:16], sc[:, 17:33], -2 * PI,
                                           sc[:, 0:16], ALU.mult, ALU.add))
                vbar(v)
                mi["wq"] = VC["n"]
                v.wait_ge(ss_s, mi["sinq"])
                vop(v.tensor_tensor_scan(
                    sc[:, 17:33], sc[:, 33:49], sc[:, 33:49], 1.0,
                    ALU.mult, ALU.bypass))
                vbar(v)
                mi["q"] = VC["n"]
                if i == 0:
                    for c in range(6, 10):
                        emit_split(c)
                v.wait_ge(tn_s, mi["m12"])
                vop(v.scalar_tensor_tensor(B[:], B[:], st_[:, C_S0K:C_S0K + 1],
                                           pm1[:], ALU.mult, ALU.add))
                vbar(v)
                vop(v.tensor_scalar(B[:], B[:], st_[:, C_M0K:C_M0K + 1],
                                    None, ALU.add))
                vbar(v)
                mi["s"] = VC["n"]
                def emit_hl(g):
                    hp = g % 2
                    v.wait_ge(ss_s, mi[f"relu{g}"])
                    if g >= 1:
                        v.wait_ge(tn_s, mi[f"m4lo{g - 1}"])
                    vop(v.scalar_tensor_tensor(
                        hl[:], pm3[:, 512 * hp:512 * (hp + 1)], 0.0, hh[:],
                        ALU.max, ALU.subtract))
                    vbar(v)
                    mi[f"hl{g}"] = VC["n"]

                if i == 0:
                    for c in range(10, 14):
                        emit_split(c)
                emit_hl(0)
                if i == 0:
                    for c in range(14, 18):
                        emit_split(c)
                emit_hl(1)
                if i + 1 < NT:
                    v.wait_ge(s_x, 16 * (i + 2))
                    v.bn_stats(st_[:, 0:6], Bn[:, 0:512])
                    vop(v.bn_stats(st_[:, 6:12], Bn[:, 512:1024]))
                    vbar(v)
                    vop(v.bn_aggr(sc[:, 45:47], st_[:, 0:12]))
                    vbar(v)
                if i == 0:
                    for c in range(18, 22):
                        emit_split(c)
                emit_hl(2)
                if i == 0:
                    for c in range(22, 26):
                        emit_split(c)
                emit_hl(3)
                if i + 1 < NT:
                    rstd_chain(v, sc[:, 46:47], sc[:, 47:48])
                if i == 0:
                    for c in range(26, 30):
                        emit_split(c)
                emit_hl(4)
                if i == 0:
                    for c in range(30, 32):
                        emit_split(c)
                emit_hl(5)
                if i + 1 < NT:
                    vop(v.tensor_scalar(Bn[:], Bn[:], sc[:, 45:46],
                                        sc[:, 47:48], ALU.subtract, ALU.mult))
                    vbar(v)
                    M[i + 1]["xn"] = VC["n"]
                    v.tensor_copy(st_[:, C_M0K:C_M0K + 1], sc[:, 45:46])
                    vop(v.tensor_copy(st_[:, C_S0K:C_S0K + 1],
                                      st_[:, C_SIG:C_SIG + 1]))
                    vbar(v)
                emit_hl(6)
                emit_hl(7)
                v.wait_ge(tn_s, mi["m4"])
                if i >= 1:
                    v.wait_ge(s_st, 16 * i)
                vop(v.scalar_tensor_tensor(abf[:], cbf[:], rstd, pm4[:],
                                           ALU.mult, ALU.add))
                vbar(v)
                mi["z2"] = VC["n"]
                if i + 1 < NT:
                    emit_masks(v, i + 1, Bn)
                ln_stats(v, abf, st_[:, C_MEAN:C_MEAN + 2], st_[:, C_VAR:C_VAR + 1], rstd)
                if i == 0:
                    v.wait_ge(s_wy, 7 * 16)
                vop(v.scalar_tensor_tensor(abf[:], abf[:], mean, g2[:],
                                           ALU.subtract, ALU.mult))
                vbar(v)
                v.wait_ge(s_b2, 16 * (i + 1))
                vop(v.scalar_tensor_tensor(abf[:], abf[:], rstd, B[:],
                                           ALU.mult, ALU.add))
                vbar(v)
                mi["out"] = VC["n"]

        @block.gpsimd
        def _(g):
            for i in range(NT):
                B_t = buf0 if i % 2 == 0 else buf1
                g.wait_ge(s_vc, M[i]["s"])
                g.dma_start(out=B_t[:], in_=be2_d,
                            accum_op=ALU.add).then_inc(s_b2, 16)

        @block.scalar
        def _(s):
            for i in range(NT):
                mi = M[i]
                s.wait_ge(s_vc, mi["wrap"])
                s.activation(cbf[:], cbf[:], AF.Sin,
                             bias=st_[:, C_PI2:C_PI2 + 1]).then_inc(ss_s, 1)
                for k in range(8):
                    s.wait_ge(tn_s, mi[f"tr{k}"])
                    src = ptrA if k % 2 == 0 else ptrB
                    dst = xnT2 if k % 2 == 0 else xnT
                    s.copy(dst[:], src[:]).then_inc(ss_s, 1)
                s.wait_ge(s_vc, mi["wq"])
                s.activation(sc[:, 33:49], sc[:, 0:16], AF.Sin,
                             bias=st_[:, C_PI2:C_PI2 + 1]).then_inc(ss_s, 1)
                s.wait_ge(tn_s, mi["qtr"])
                s.copy(qt[0:17, :], ptrA[0:17, :]).then_inc(ss_s, 1)
                for g in range(8):
                    s.wait_ge(tn_s, mi[f"m3g{g}"])
                    if g >= 1:
                        s.wait_ge(tn_s, mi[f"m4hi{g - 1}"])
                        s.wait_ge(s_vc, mi[f"hl{g - 1}"])
                    elif i >= 1:
                        s.wait_ge(tn_s, M[i - 1]["m4hi7"])
                        s.wait_ge(s_vc, M[i - 1]["hl7"])
                    h = g % 2
                    s.activation(hh[:], pm3[:, 512 * h:512 * (h + 1)],
                                 AF.Relu).then_inc(ss_s, 1)

        @block.tensor
        def _(t):
            t.wait_ge(sg_s, 1)
            for i in range(NT):
                mi = M[i]
                Bx = buf0 if i % 2 == 0 else buf1
                t.wait_ge(s_vc, mi["xn"])
                if i >= 1:
                    t.wait_ge(ss_s, M[i - 1]["qev"])
                    t.wait_ge(ss_s, M[i - 1]["ev7"])
                    t.wait_ge(s_vc, M[i - 1]["s"])

                def tr(k):
                    dst = ptrA if k % 2 == 0 else ptrB
                    if k >= 2:
                        t.wait_ge(ss_s, mi[f"ev{k - 2}"])
                    t.transpose(dst[:], Bx[:, P * k:P * (k + 1)],
                                ident[:]).then_inc(tn_s, 1)

                def m1mm(k):
                    t.wait_ge(ss_s, mi[f"ev{k}"])
                    xs = xnT2 if k % 2 == 0 else xnT
                    t.matmul(pm1[:, 0:512], xs[:], wft[:, D * k:D * k + 512],
                             start=(k == 0), stop=(k == 7))
                    t.matmul(pm1[:, 512:1024], xs[:],
                             wft[:, D * k + 512:D * (k + 1)],
                             start=(k == 0), stop=(k == 7)).then_inc(tn_s, 1)

                tr(0)
                tr(1)
                if i == 0:
                    t.wait_ge(s_vc, FB["fold"])
                else:
                    t.wait_ge(s_vc, M[i - 1]["z2"])
                t.matmul(pm4[:, 0:512], qt[0:1, :], cb[0:1, 0:512],
                         start=True, stop=False)
                t.matmul(pm4[:, 512:1024], qt[0:1, :], cb[0:1, 512:1024],
                         start=True, stop=False)
                for k in range(8):
                    m1mm(k)
                    if k + 2 < 8:
                        tr(k + 2)
                t.wait_ge(s_vc, mi["q"])
                t.transpose(ptrA[0:17, :], sc[:, 16:33],
                            ident[:]).then_inc(tn_s, 1)

                def m3grp(g):
                    h = g % 2
                    for r in range(4):
                        c = 4 * g + r
                        mm = t.matmul(
                            pm3[:, 512 * h + 128 * r:512 * h + 128 * (r + 1)],
                            w1a[0:17, 128 * c:128 * (c + 1)], qt[0:17, :],
                            start=(r == 0), stop=(r == 3))
                    mm.then_inc(tn_s, 1)

                t.wait_ge(ss_s, mi["qev"])
                if i >= 1:
                    t.wait_ge(ss_s, M[i - 1]["relu7"])
                    t.wait_ge(s_vc, M[i - 1]["hl7"])
                m3grp(0)
                m3grp(1)
                for g in range(8):
                    t.wait_ge(ss_s, mi[f"relu{g}"])
                    if i == 0:
                        t.wait_ge(s_vc, LO_BAR[4 * g + 3])
                    for r in range(4):
                        c = 4 * g + r
                        hs = hh[:, 128 * r:128 * (r + 1)]
                        t.matmul(pm4[:, 0:512], hs, w2th[:, D * c:D * c + 512],
                                 start=False, stop=False)
                        t.matmul(pm4[:, 512:1024], hs,
                                 w2th[:, D * c + 512:D * (c + 1)],
                                 start=False, stop=False)
                        t.matmul(pm4[:, 0:512], hs, w2tl[:, D * c:D * c + 512],
                                 start=False, stop=False)
                        mm2 = t.matmul(pm4[:, 512:1024], hs,
                                       w2tl[:, D * c + 512:D * (c + 1)],
                                       start=False, stop=False)
                    mm2.then_inc(tn_s, 1)
                    t.wait_ge(s_vc, mi[f"hl{g}"])
                    for r in range(4):
                        c = 4 * g + r
                        ls = hl[:, 128 * r:128 * (r + 1)]
                        t.matmul(pm4[:, 0:512], ls, w2th[:, D * c:D * c + 512],
                                 start=False, stop=(g == 7 and r == 3))
                        mm3 = t.matmul(pm4[:, 512:1024], ls,
                                       w2th[:, D * c + 512:D * (c + 1)],
                                       start=False, stop=(g == 7 and r == 3))
                    mm3.then_inc(tn_s, 1)
                    if g + 2 <= 7:
                        m3grp(g + 2)

        @block.sync
        def _(y):
            y.dma_start(out=buf0[:], in_=x_d[0:P, :]).then_inc(s_x, 16)
            for c in range(32):
                dh = stgh[c][:] if c < 31 else abf[:, 0:512]
                dl = stgl[c][:] if c < 31 else abf[:, 512:1024]
                if c == 6:
                    y.wait_ge(s_vc, FB["fold"])
                    y.dma_start(out=buf1[:],
                                in_=x_d[P:2 * P, :]).then_inc(s_x, 16)
                if c >= 6:
                    y.wait_ge(s_vc, LO_BAR[c - 6])
                y.dma_start(out=dh,
                            in_=w2ta_d[P * c:P * (c + 1), :]).then_inc(
                    s_ck[c], 16)
                y.dma_start(out=dl,
                            in_=w2tb_d[P * c:P * (c + 1), :]).then_inc(
                    s_ck[c], 16)
            for i in range(NT):
                if 1 <= i < NT - 1:
                    Bn = buf1 if i % 2 == 0 else buf0
                    y.wait_ge(s_vc, M[i - 1]["out"])
                    y.wait_ge(tn_s, M[i - 1]["tr7"])
                    y.dma_start(out=Bn[:],
                                in_=x_d[P * (i + 1):P * (i + 2), :]).then_inc(
                        s_x, 16)
                y.wait_ge(s_vc, M[i]["out"])
                y.dma_start(out=out_d[P * i:P * (i + 1), :],
                            in_=abf[:]).then_inc(s_st, 16)
            y.wait_ge(s_st, 16 * NT)

    return nc


def _host_layouts(w1, b1, w2, g1v, be1v, g2v, be2v, b2v, wf, dsh):
    f32 = np.float32
    w2t = np.asarray(w2, f32).T
    w2ta = np.ascontiguousarray(w2t[:, 0:512])
    w2tb = np.ascontiguousarray(w2t[:, 512:1024])
    wft = np.ascontiguousarray(
        np.asarray(wf, f32).T.reshape(8, 128, 1024)
        .transpose(1, 0, 2).reshape(128, 8 * 1024))
    w1a = np.ascontiguousarray(np.concatenate(
        [np.asarray(b1, f32)[None, :], np.asarray(w1, f32).T], axis=0))
    d = np.asarray(dsh, f32)
    dsh2 = np.ascontiguousarray(np.concatenate([d, d], axis=0))
    rep = lambda v: np.ascontiguousarray(
        np.broadcast_to(np.asarray(v, f32)[None, :], (P, D)))
    return {"w2ta": w2ta, "w2tb": w2tb, "wft": wft, "w1a": w1a, "dsh2": dsh2,
            "g1r": rep(g1v), "g2r": rep(g2v), "be2r": rep(be2v),
            "be1r": rep(be1v), "b2r": rep(b2v),
            "b116r": np.ascontiguousarray(
                np.broadcast_to(np.asarray(be1v, f32)[None, :16], (P, 16)))}


def kernel(L0_attn_x, L0_attn_attn_circuit_params, L0_attn_ffn_circuit_params,
           L0_attn_w1, L0_attn_b1, L0_attn_w2, L0_attn_b2,
           L0_attn_gamma1, L0_attn_beta1, L0_attn_gamma2, L0_attn_beta2,
           L0_ffn_weight, L0_ffn_disha):
    x = np.ascontiguousarray(np.asarray(L0_attn_x, np.float32).reshape(-1, D))
    weights = _host_layouts(L0_attn_w1, L0_attn_b1, L0_attn_w2,
                            L0_attn_gamma1, L0_attn_beta1,
                            L0_attn_gamma2, L0_attn_beta2,
                            L0_attn_b2, L0_ffn_weight, L0_ffn_disha)
    rows = x.shape[0] // NCORES
    in_maps = []
    for c in range(NCORES):
        m = dict(weights)
        m["x"] = np.ascontiguousarray(x[rows * c:rows * (c + 1)])
        in_maps.append(m)
    nc = build_core()
    res = run_bass_kernel_spmd(nc, in_maps, core_ids=list(range(NCORES)))
    out = np.concatenate([np.asarray(r["out"]) for r in res.results], axis=0)
    return np.ascontiguousarray(out.reshape(8, 4096, 1024).astype(np.float32))


# version 27
# speedup vs baseline: 1.0273x; 1.0006x over previous
import math
import numpy as np
import concourse.bass as bass
import concourse.mybir as mybir
from concourse.bass_utils import run_bass_kernel_spmd

F32 = mybir.dt.float32
BF16 = mybir.dt.bfloat16
AF = mybir.ActivationFunctionType
ALU = mybir.AluOpType
AX = mybir.AxisListType

P = 128
D = 1024
FF = 4096
NT = 32
NCORES = 8
PI = math.pi
EPS = 1e-5

C_MEAN, C_VAR = 12, 13
C_SIG, C_HV, C_RC, C_SH = 0, 1, 2, 3
C_RSTD = 14
C_PI2 = 15
C_M0K, C_S0K = 16, 17

S_PER = 19
T_PER = 41
_TR_NUM = {0: 1, 1: 2, 2: 4, 3: 6, 4: 8, 5: 10, 6: 12, 7: 14}
_MM_NUM = {0: 3, 1: 5, 2: 7, 3: 9, 4: 11, 5: 13, 6: 15, 7: 16}


def _register_milestones():
    M = [dict() for _ in range(NT)]
    for i in range(NT):
        b = S_PER * i
        M[i]["sin"] = b + 1
        for k in range(8):
            M[i][f"ev{k}"] = b + 2 + k
        M[i]["sinq"] = b + 10
        M[i]["qev"] = b + 11
        for g in range(8):
            M[i][f"relu{g}"] = b + 12 + g
        t = T_PER * i
        for k in range(8):
            M[i][f"tr{k}"] = t + _TR_NUM[k]
            M[i][f"mm{k}"] = t + _MM_NUM[k]
        M[i]["qtr"] = t + 17
        for g in range(8):
            M[i][f"m3g{g}"] = t + (18 + g if g <= 1 else 22 + 3 * (g - 2))
            M[i][f"m4hi{g}"] = t + (20 + 3 * g if g <= 6 else 40)
            M[i][f"m4lo{g}"] = t + (21 + 3 * g if g <= 6 else 41)
        M[i]["m12"] = t + 16
        M[i]["m4"] = t + 41
    return M


def build_core():
    nc = bass.Bass("TRN2", target_bir_lowering=False)
    x_d = nc.dram_tensor("x", [NT * P, D], F32, kind="ExternalInput").ap()
    out_d = nc.dram_tensor("out", [NT * P, D], F32, kind="ExternalOutput").ap()
    w2ta_d = nc.dram_tensor("w2ta", [FF, 512], F32, kind="ExternalInput").ap()
    w2tb_d = nc.dram_tensor("w2tb", [FF, 512], F32, kind="ExternalInput").ap()
    wft_d = nc.dram_tensor("wft", [P, (D // P) * D], F32, kind="ExternalInput").ap()
    w1a_d = nc.dram_tensor("w1a", [17, FF], F32, kind="ExternalInput").ap()
    dsh2_d = nc.dram_tensor("dsh2", [P, D], F32, kind="ExternalInput").ap()
    g1_d = nc.dram_tensor("g1r", [P, D], F32, kind="ExternalInput").ap()
    g2_d = nc.dram_tensor("g2r", [P, D], F32, kind="ExternalInput").ap()
    be2_d = nc.dram_tensor("be2r", [P, D], F32, kind="ExternalInput").ap()
    be1_d = nc.dram_tensor("be1r", [P, D], F32, kind="ExternalInput").ap()
    b2_d = nc.dram_tensor("b2r", [P, D], F32, kind="ExternalInput").ap()
    b116_d = nc.dram_tensor("b116r", [P, 16], F32, kind="ExternalInput").ap()

    from contextlib import ExitStack
    es = ExitStack()
    with es:
        sb = lambda n, s: es.enter_context(nc.sbuf_tensor(n, s, F32))
        sbh = lambda n, s: es.enter_context(nc.sbuf_tensor(n, s, BF16))
        psb = lambda n, s: es.enter_context(nc.psum_tensor(n, s, F32))
        sem = lambda n: es.enter_context(nc.semaphore(n))

        w2th_off = (nc.sbuf_base + 31) & ~31
        w2th = sbh("w2th_s", [P, (FF // P) * D])
        w2tl_off = (nc.sbuf_base + 31) & ~31
        w2tl = sbh("w2tl_s", [P, (FF // P) * D])
        wft = sb("wft_s", [P, (D // P) * D])
        w1a = sb("w1a_s", [17, FF])
        g1 = sb("g1_s", [P, D])
        g2 = sb("g2_s", [P, D])
        cb = sb("cb_s", [P, D])
        b116 = sb("b116_s", [P, 16])
        ident = sb("ident_s", [P, P])
        qt = sb("qt_s", [17, P])
        buf0 = sb("buf0_s", [P, D])
        buf1 = sb("buf1_s", [P, D])
        cbf = sb("c_s", [P, D])
        abf = sb("a_s", [P, D])
        hh_off = (nc.sbuf_base + 31) & ~31
        hh = sbh("hh_s", [P, 512])
        hl = sbh("hl_s", [P, 512])
        xnT = sb("xnT_s", [P, P])
        small = sb("small_s", [P, 67])
        xnT2 = nc.alloc_sbuf_tensor_at("xnT2_s", [P, P], F32, offset=hh_off)
        stgh = [nc.alloc_sbuf_tensor_at(f"stgh{c}", [P, 512], F32,
                                        offset=w2th_off + 2048 * (c + 1))
                for c in range(31)]
        stgl = [nc.alloc_sbuf_tensor_at(f"stgl{c}", [P, 512], F32,
                                        offset=w2tl_off + 2048 * (c + 1))
                for c in range(31)]
        sc = small[:, 0:49]
        st_ = small[:, 49:67]

        pm1 = psb("pm1", [P, 1024])
        pm4 = psb("pm4", [P, 1024])
        pm3 = psb("pm3", [P, 1024])
        ptrA = psb("ptrA", [P, 128])
        ptrB = psb("ptrB", [P, 128])

        s_x = sem("s_x")
        s_st = sem("s_st")
        s_wy = sem("s_wy")
        s_ck = [sem(f"s_ck{c}") for c in range(32)]
        s_vc = sem("s_vc")
        s_b2 = sem("s_b2")
        s_ws = sem("s_ws")
        ss_s = sem("ss")
        tn_s = sem("tn")
        sg_s = sem("sg")

        block = es.enter_context(nc.Block())
        M = _register_milestones()

        VC = {"n": 0, "last": None}
        LO_BAR = {}
        FB = {}

        def vop(inst):
            VC["last"] = inst
            return inst

        def vbar(v):
            if VC["last"] is not None:
                VC["n"] += 1
                VC["last"].then_inc(s_vc, 1)
                v.wait_ge(s_vc, VC["n"])
                VC["last"] = None

        @block.scalar
        def _(s):
            s.dma_start(out=buf1[:], in_=b2_d).then_inc(s_ws, 16)

        mean = st_[:, C_MEAN:C_MEAN + 1]
        rstd = st_[:, C_RSTD:C_RSTD + 1]

        def rstd_chain(v, var, out):
            sig = st_[:, C_SIG:C_SIG + 1]
            hv = st_[:, C_HV:C_HV + 1]
            rc = st_[:, C_RC:C_RC + 1]
            sh = st_[:, C_SH:C_SH + 1]
            v.tensor_scalar(sig, var, 1.0 + EPS, 0.5, ALU.add, ALU.mult)
            vop(v.tensor_scalar(hv, var, EPS, 0.5, ALU.add, ALU.mult))
            vbar(v)
            for _ in range(3):
                v.reciprocal(rc, sig)
                vop(v.tensor_scalar(sh, sig, 0.5, None, ALU.mult))
                vbar(v)
                vop(v.scalar_tensor_tensor(sig, rc, hv, sh, ALU.mult, ALU.add))
                vbar(v)
            vop(v.reciprocal(out, sig))
            vbar(v)

        def ln_stats(v, src, mv_pair, var, out):
            v.bn_stats(st_[:, 0:6], src[:, 0:512])
            vop(v.bn_stats(st_[:, 6:12], src[:, 512:1024]))
            vbar(v)
            vop(v.bn_aggr(mv_pair, st_[:, 0:12]))
            vbar(v)
            rstd_chain(v, var, out)

        def emit_masks(v, i, B):
            vop(v.tensor_scalar(cbf[:], B[:], PI / 2, None, ALU.is_gt))
            vbar(v)
            vop(v.scalar_tensor_tensor(cbf[:], B[:], -3 * PI / 2, cbf[:],
                                       ALU.is_lt, ALU.subtract))
            vbar(v)
            vop(v.scalar_tensor_tensor(cbf[:], cbf[:], 2 * PI, B[:],
                                       ALU.mult, ALU.add))
            vbar(v)
            M[i]["wrap"] = VC["n"]

        @block.vector
        def _(v):
            v.memset(ident[:], 1.0)
            v.memset(st_[:, C_PI2:C_PI2 + 1], PI / 2)
            v.memset(sc[:, 16:17], 1.0)
            vop(v.memset(qt[0:17, :], 1.0))
            vbar(v)

            def emit_split_p(c):
                sh = stgh[c][:] if c < 31 else abf[:, 0:512]
                sl = stgl[c][:] if c < 31 else abf[:, 512:1024]
                v.wait_ge(s_ck[c], 32)
                v.tensor_copy(w2th[:, D * c:D * c + 512], sh)
                vop(v.tensor_copy(w2th[:, D * c + 512:D * (c + 1)], sl))
                vbar(v)
                v.tensor_tensor(w2tl[:, D * c:D * c + 512], sh,
                                w2th[:, D * c:D * c + 512], ALU.subtract)
                vop(v.tensor_tensor(w2tl[:, D * c + 512:D * (c + 1)], sl,
                                    w2th[:, D * c + 512:D * (c + 1)],
                                    ALU.subtract))
                vbar(v)
                LO_BAR[c] = VC["n"]

            v.wait_ge(s_x, 16)
            ln_stats(v, buf0, sc[:, 45:47], sc[:, 46:47], sc[:, 47:48])
            vop(v.tensor_scalar(buf0[:], buf0[:], sc[:, 45:46], sc[:, 47:48],
                                ALU.subtract, ALU.mult))
            vbar(v)
            M[0]["xn"] = VC["n"]
            v.tensor_copy(st_[:, C_M0K:C_M0K + 1], sc[:, 45:46])
            vop(v.tensor_copy(st_[:, C_S0K:C_S0K + 1], st_[:, C_SIG:C_SIG + 1]))
            vbar(v)
            emit_masks(v, 0, buf0)
            for c in range(6):
                emit_split_p(c)
            v.wait_ge(s_wy, 6 * 16)
            v.wait_ge(s_ws, 16)
            for k in range(D // P):
                v.tensor_tensor(wft[:, D * k:D * (k + 1)],
                                wft[:, D * k:D * (k + 1)], g2[:], ALU.add)
            vop(v.tensor_tensor(cb[:], cb[:], buf1[:], ALU.add))
            vbar(v)
            FB["fold"] = VC["n"]

        @block.gpsimd
        def _(g):
            g.dma_start(out=wft[:], in_=wft_d).then_inc(s_wy, 16)
            g.dma_start(out=w1a[:], in_=w1a_d).then_inc(s_wy, 16)
            g.dma_start(out=g1[:], in_=g1_d).then_inc(s_wy, 16)
            g.dma_start(out=cb[:], in_=be1_d).then_inc(s_wy, 16)
            g.dma_start(out=b116[:], in_=b116_d).then_inc(s_wy, 16)
            g.dma_start(out=g2[:], in_=dsh2_d).then_inc(s_wy, 16)
            g.wait_ge(s_vc, 1)
            g.affine_select(ident[:], ident[:], [[1, P]], ALU.is_equal, 0.0,
                            base=0, channel_multiplier=-1).then_inc(sg_s, 1)
            g.wait_ge(s_vc, FB["fold"])
            g.dma_start(out=g2[:], in_=g2_d).then_inc(s_wy, 16)

        @block.vector
        def _(v):
            def emit_split(c):
                sh = stgh[c][:] if c < 31 else abf[:, 0:512]
                sl = stgl[c][:] if c < 31 else abf[:, 512:1024]
                v.wait_ge(s_ck[c], 32)
                v.tensor_copy(w2th[:, D * c:D * c + 512], sh)
                vop(v.tensor_copy(w2th[:, D * c + 512:D * (c + 1)], sl))
                vbar(v)
                v.tensor_tensor(w2tl[:, D * c:D * c + 512], sh,
                                w2th[:, D * c:D * c + 512], ALU.subtract)
                vop(v.tensor_tensor(w2tl[:, D * c + 512:D * (c + 1)], sl,
                                    w2th[:, D * c + 512:D * (c + 1)],
                                    ALU.subtract))
                vbar(v)
                LO_BAR[c] = VC["n"]

            for i in range(NT):
                mi = M[i]
                B = buf0 if i % 2 == 0 else buf1
                Bn = buf1 if i % 2 == 0 else buf0
                v.wait_ge(ss_s, mi["sin"])
                for g in range(16):
                    vop(v.tensor_tensor_scan(cbf[:, 64 * g:64 * (g + 1)],
                                             cbf[:, 64 * g:64 * (g + 1)],
                                             cbf[:, 64 * g:64 * (g + 1)],
                                             1.0, ALU.mult, ALU.bypass))
                vbar(v)
                vop(v.tensor_tensor(cbf[:], B[:], cbf[:], ALU.add))
                vbar(v)
                mi["z1"] = VC["n"]
                ln_stats(v, cbf, st_[:, C_MEAN:C_MEAN + 2], st_[:, C_VAR:C_VAR + 1], rstd)
                vop(v.scalar_tensor_tensor(cbf[:], cbf[:], mean, g1[:],
                                           ALU.subtract, ALU.mult))
                vbar(v)
                mi["x1"] = VC["n"]
                vop(v.tensor_scalar(sc[:, 0:16], cbf[:, 0:16], rstd, None,
                                    ALU.mult))
                vbar(v)
                vop(v.tensor_tensor(sc[:, 0:16], sc[:, 0:16], b116[:],
                                    ALU.add))
                vbar(v)
                v.tensor_scalar(sc[:, 17:33], sc[:, 0:16], PI / 2, PI,
                                ALU.add, ALU.is_gt)
                vop(v.tensor_scalar(sc[:, 33:49], sc[:, 0:16], PI / 2, -PI,
                                    ALU.add, ALU.is_lt))
                vbar(v)
                vop(v.tensor_tensor(sc[:, 17:33], sc[:, 17:33], sc[:, 33:49],
                                    ALU.subtract))
                vbar(v)
                vop(v.scalar_tensor_tensor(sc[:, 0:16], sc[:, 17:33], -2 * PI,
                                           sc[:, 0:16], ALU.mult, ALU.add))
                vbar(v)
                mi["wq"] = VC["n"]
                v.wait_ge(ss_s, mi["sinq"])
                vop(v.tensor_tensor_scan(
                    sc[:, 17:33], sc[:, 33:49], sc[:, 33:49], 1.0,
                    ALU.mult, ALU.bypass))
                vbar(v)
                mi["q"] = VC["n"]
                if i == 0:
                    for c in range(6, 10):
                        emit_split(c)
                v.wait_ge(tn_s, mi["m12"])
                vop(v.scalar_tensor_tensor(B[:], B[:], st_[:, C_S0K:C_S0K + 1],
                                           pm1[:], ALU.mult, ALU.add))
                vbar(v)
                vop(v.tensor_scalar(B[:], B[:], st_[:, C_M0K:C_M0K + 1],
                                    None, ALU.add))
                vbar(v)
                mi["s"] = VC["n"]
                def emit_hl(g):
                    hp = g % 2
                    v.wait_ge(ss_s, mi[f"relu{g}"])
                    if g >= 1:
                        v.wait_ge(tn_s, mi[f"m4lo{g - 1}"])
                    vop(v.scalar_tensor_tensor(
                        hl[:], pm3[:, 512 * hp:512 * (hp + 1)], 0.0, hh[:],
                        ALU.max, ALU.subtract))
                    vbar(v)
                    mi[f"hl{g}"] = VC["n"]

                if i == 0:
                    for c in range(10, 14):
                        emit_split(c)
                emit_hl(0)
                if i == 0:
                    for c in range(14, 18):
                        emit_split(c)
                emit_hl(1)
                if i + 1 < NT:
                    v.wait_ge(s_x, 16 * (i + 2))
                    v.bn_stats(st_[:, 0:6], Bn[:, 0:512])
                    vop(v.bn_stats(st_[:, 6:12], Bn[:, 512:1024]))
                    vbar(v)
                    vop(v.bn_aggr(sc[:, 45:47], st_[:, 0:12]))
                    vbar(v)
                if i == 0:
                    for c in range(18, 22):
                        emit_split(c)
                emit_hl(2)
                if i == 0:
                    for c in range(22, 26):
                        emit_split(c)
                emit_hl(3)
                if i + 1 < NT:
                    rstd_chain(v, sc[:, 46:47], sc[:, 47:48])
                if i == 0:
                    for c in range(26, 30):
                        emit_split(c)
                emit_hl(4)
                if i == 0:
                    for c in range(30, 32):
                        emit_split(c)
                emit_hl(5)
                if i + 1 < NT:
                    vop(v.tensor_scalar(Bn[:], Bn[:], sc[:, 45:46],
                                        sc[:, 47:48], ALU.subtract, ALU.mult))
                    vbar(v)
                    M[i + 1]["xn"] = VC["n"]
                    v.tensor_copy(st_[:, C_M0K:C_M0K + 1], sc[:, 45:46])
                    vop(v.tensor_copy(st_[:, C_S0K:C_S0K + 1],
                                      st_[:, C_SIG:C_SIG + 1]))
                    vbar(v)
                emit_hl(6)
                emit_hl(7)
                v.wait_ge(tn_s, mi["m4"])
                if i >= 1:
                    v.wait_ge(s_st, 16 * i)
                vop(v.scalar_tensor_tensor(abf[:], cbf[:], rstd, pm4[:],
                                           ALU.mult, ALU.add))
                vbar(v)
                mi["z2"] = VC["n"]
                if i + 1 < NT:
                    emit_masks(v, i + 1, Bn)
                ln_stats(v, abf, st_[:, C_MEAN:C_MEAN + 2], st_[:, C_VAR:C_VAR + 1], rstd)
                if i == 0:
                    v.wait_ge(s_wy, 7 * 16)
                vop(v.scalar_tensor_tensor(abf[:], abf[:], mean, g2[:],
                                           ALU.subtract, ALU.mult))
                vbar(v)
                v.wait_ge(s_b2, 16 * (i + 1))
                vop(v.scalar_tensor_tensor(abf[:], abf[:], rstd, B[:],
                                           ALU.mult, ALU.add))
                vbar(v)
                mi["out"] = VC["n"]

        @block.gpsimd
        def _(g):
            for i in range(NT):
                B_t = buf0 if i % 2 == 0 else buf1
                g.wait_ge(s_vc, M[i]["s"])
                g.dma_start(out=B_t[:], in_=be2_d,
                            accum_op=ALU.add).then_inc(s_b2, 16)

        @block.scalar
        def _(s):
            for i in range(NT):
                mi = M[i]
                s.wait_ge(s_vc, mi["wrap"])
                s.activation(cbf[:], cbf[:], AF.Sin,
                             bias=st_[:, C_PI2:C_PI2 + 1]).then_inc(ss_s, 1)
                for k in range(8):
                    s.wait_ge(tn_s, mi[f"tr{k}"])
                    src = ptrA if k % 2 == 0 else ptrB
                    dst = xnT2 if k % 2 == 0 else xnT
                    s.copy(dst[:], src[:]).then_inc(ss_s, 1)
                s.wait_ge(s_vc, mi["wq"])
                s.activation(sc[:, 33:49], sc[:, 0:16], AF.Sin,
                             bias=st_[:, C_PI2:C_PI2 + 1]).then_inc(ss_s, 1)
                s.wait_ge(tn_s, mi["qtr"])
                s.copy(qt[0:17, :], ptrA[0:17, :]).then_inc(ss_s, 1)
                for g in range(8):
                    s.wait_ge(tn_s, mi[f"m3g{g}"])
                    if g >= 1:
                        s.wait_ge(tn_s, mi[f"m4hi{g - 1}"])
                        s.wait_ge(s_vc, mi[f"hl{g - 1}"])
                    elif i >= 1:
                        s.wait_ge(tn_s, M[i - 1]["m4hi7"])
                        s.wait_ge(s_vc, M[i - 1]["hl7"])
                    h = g % 2
                    s.activation(hh[:], pm3[:, 512 * h:512 * (h + 1)],
                                 AF.Relu).then_inc(ss_s, 1)

        @block.tensor
        def _(t):
            t.wait_ge(sg_s, 1)
            for i in range(NT):
                mi = M[i]
                Bx = buf0 if i % 2 == 0 else buf1
                t.wait_ge(s_vc, mi["xn"])
                if i >= 1:
                    t.wait_ge(ss_s, M[i - 1]["qev"])
                    t.wait_ge(ss_s, M[i - 1]["ev7"])
                    t.wait_ge(s_vc, M[i - 1]["s"])

                def tr(k):
                    dst = ptrA if k % 2 == 0 else ptrB
                    if k >= 2:
                        t.wait_ge(ss_s, mi[f"ev{k - 2}"])
                    t.transpose(dst[:], Bx[:, P * k:P * (k + 1)],
                                ident[:]).then_inc(tn_s, 1)

                def m1mm(k):
                    t.wait_ge(ss_s, mi[f"ev{k}"])
                    xs = xnT2 if k % 2 == 0 else xnT
                    t.matmul(pm1[:, 0:512], xs[:], wft[:, D * k:D * k + 512],
                             start=(k == 0), stop=(k == 7))
                    t.matmul(pm1[:, 512:1024], xs[:],
                             wft[:, D * k + 512:D * (k + 1)],
                             start=(k == 0), stop=(k == 7)).then_inc(tn_s, 1)

                tr(0)
                tr(1)
                if i == 0:
                    t.wait_ge(s_vc, FB["fold"])
                else:
                    t.wait_ge(s_vc, M[i - 1]["z2"])
                t.matmul(pm4[:, 0:512], qt[0:1, :], cb[0:1, 0:512],
                         start=True, stop=False)
                t.matmul(pm4[:, 512:1024], qt[0:1, :], cb[0:1, 512:1024],
                         start=True, stop=False)
                for k in range(8):
                    m1mm(k)
                    if k + 2 < 8:
                        tr(k + 2)
                t.wait_ge(s_vc, mi["q"])
                t.transpose(ptrA[0:17, :], sc[:, 16:33],
                            ident[:]).then_inc(tn_s, 1)

                def m3grp(g):
                    h = g % 2
                    for r in range(4):
                        c = 4 * g + r
                        mm = t.matmul(
                            pm3[:, 512 * h + 128 * r:512 * h + 128 * (r + 1)],
                            w1a[0:17, 128 * c:128 * (c + 1)], qt[0:17, :],
                            start=(r == 0), stop=(r == 3))
                    mm.then_inc(tn_s, 1)

                t.wait_ge(ss_s, mi["qev"])
                if i >= 1:
                    t.wait_ge(ss_s, M[i - 1]["relu7"])
                    t.wait_ge(s_vc, M[i - 1]["hl7"])
                m3grp(0)
                m3grp(1)
                for g in range(8):
                    t.wait_ge(ss_s, mi[f"relu{g}"])
                    if i == 0:
                        t.wait_ge(s_vc, LO_BAR[4 * g + 3])
                    for r in range(4):
                        c = 4 * g + r
                        hs = hh[:, 128 * r:128 * (r + 1)]
                        t.matmul(pm4[:, 0:512], hs, w2th[:, D * c:D * c + 512],
                                 start=False, stop=False)
                        t.matmul(pm4[:, 512:1024], hs,
                                 w2th[:, D * c + 512:D * (c + 1)],
                                 start=False, stop=False)
                        t.matmul(pm4[:, 0:512], hs, w2tl[:, D * c:D * c + 512],
                                 start=False, stop=False)
                        mm2 = t.matmul(pm4[:, 512:1024], hs,
                                       w2tl[:, D * c + 512:D * (c + 1)],
                                       start=False, stop=False)
                    mm2.then_inc(tn_s, 1)
                    t.wait_ge(s_vc, mi[f"hl{g}"])
                    for r in range(4):
                        c = 4 * g + r
                        ls = hl[:, 128 * r:128 * (r + 1)]
                        t.matmul(pm4[:, 0:512], ls, w2th[:, D * c:D * c + 512],
                                 start=False, stop=(g == 7 and r == 3))
                        mm3 = t.matmul(pm4[:, 512:1024], ls,
                                       w2th[:, D * c + 512:D * (c + 1)],
                                       start=False, stop=(g == 7 and r == 3))
                    mm3.then_inc(tn_s, 1)
                    if g + 2 <= 7:
                        m3grp(g + 2)

        @block.sync
        def _(y):
            y.dma_start(out=buf0[:], in_=x_d[0:P, :]).then_inc(s_x, 16)
            for c in range(32):
                dh = stgh[c][:] if c < 31 else abf[:, 0:512]
                dl = stgl[c][:] if c < 31 else abf[:, 512:1024]
                if c == 6:
                    y.wait_ge(s_vc, FB["fold"])
                    y.dma_start(out=buf1[:],
                                in_=x_d[P:2 * P, :]).then_inc(s_x, 16)
                if c >= 10:
                    y.wait_ge(s_vc, LO_BAR[c - 10])
                y.dma_start(out=dh,
                            in_=w2ta_d[P * c:P * (c + 1), :]).then_inc(
                    s_ck[c], 16)
                y.dma_start(out=dl,
                            in_=w2tb_d[P * c:P * (c + 1), :]).then_inc(
                    s_ck[c], 16)
            for i in range(NT):
                if 1 <= i < NT - 1:
                    Bn = buf1 if i % 2 == 0 else buf0
                    y.wait_ge(s_vc, M[i - 1]["out"])
                    y.wait_ge(tn_s, M[i - 1]["tr7"])
                    y.dma_start(out=Bn[:],
                                in_=x_d[P * (i + 1):P * (i + 2), :]).then_inc(
                        s_x, 16)
                y.wait_ge(s_vc, M[i]["out"])
                y.dma_start(out=out_d[P * i:P * (i + 1), :],
                            in_=abf[:]).then_inc(s_st, 16)
            y.wait_ge(s_st, 16 * NT)

    return nc


def _host_layouts(w1, b1, w2, g1v, be1v, g2v, be2v, b2v, wf, dsh):
    f32 = np.float32
    w2t = np.asarray(w2, f32).T
    w2ta = np.ascontiguousarray(w2t[:, 0:512])
    w2tb = np.ascontiguousarray(w2t[:, 512:1024])
    wft = np.ascontiguousarray(
        np.asarray(wf, f32).T.reshape(8, 128, 1024)
        .transpose(1, 0, 2).reshape(128, 8 * 1024))
    w1a = np.ascontiguousarray(np.concatenate(
        [np.asarray(b1, f32)[None, :], np.asarray(w1, f32).T], axis=0))
    d = np.asarray(dsh, f32)
    dsh2 = np.ascontiguousarray(np.concatenate([d, d], axis=0))
    rep = lambda v: np.ascontiguousarray(
        np.broadcast_to(np.asarray(v, f32)[None, :], (P, D)))
    return {"w2ta": w2ta, "w2tb": w2tb, "wft": wft, "w1a": w1a, "dsh2": dsh2,
            "g1r": rep(g1v), "g2r": rep(g2v), "be2r": rep(be2v),
            "be1r": rep(be1v), "b2r": rep(b2v),
            "b116r": np.ascontiguousarray(
                np.broadcast_to(np.asarray(be1v, f32)[None, :16], (P, 16)))}


def kernel(L0_attn_x, L0_attn_attn_circuit_params, L0_attn_ffn_circuit_params,
           L0_attn_w1, L0_attn_b1, L0_attn_w2, L0_attn_b2,
           L0_attn_gamma1, L0_attn_beta1, L0_attn_gamma2, L0_attn_beta2,
           L0_ffn_weight, L0_ffn_disha):
    x = np.ascontiguousarray(np.asarray(L0_attn_x, np.float32).reshape(-1, D))
    weights = _host_layouts(L0_attn_w1, L0_attn_b1, L0_attn_w2,
                            L0_attn_gamma1, L0_attn_beta1,
                            L0_attn_gamma2, L0_attn_beta2,
                            L0_attn_b2, L0_ffn_weight, L0_ffn_disha)
    rows = x.shape[0] // NCORES
    in_maps = []
    for c in range(NCORES):
        m = dict(weights)
        m["x"] = np.ascontiguousarray(x[rows * c:rows * (c + 1)])
        in_maps.append(m)
    nc = build_core()
    res = run_bass_kernel_spmd(nc, in_maps, core_ids=list(range(NCORES)))
    out = np.concatenate([np.asarray(r["out"]) for r in res.results], axis=0)
    return np.ascontiguousarray(out.reshape(8, 4096, 1024).astype(np.float32))
